# version 01 (final)
# baseline (speedup 1.0000x reference)
import os as _os2
import numpy as np
import concourse.bass as bass
import concourse.mybir as mybir
from concourse import bass_isa

F32 = mybir.dt.float32
BF16 = mybir.dt.bfloat16
F16 = mybir.dt.float16
I32 = mybir.dt.int32
AF = mybir.ActivationFunctionType
ALU = mybir.AluOpType
AX = mybir.AxisListType

B, T, D = 8, 2048, 1024
DI, DS = 2048, 16
FF = 4096
NH, DH, DE = 16, 64, 128

ENGS = ("t", "v", "s", "g", "y")


class Script:

    def __init__(self, nc):
        self.nc = nc
        self.ops = []
        self.cnt = {e: 0 for e in ENGS}

    para = False

    def rec(self, eng, fn, waits=(), delta=1):
        waits = list(waits)
        if self.para:
            waits += self.after(*ENGS)
        self.ops.append((eng, fn, waits, delta))
        self.cnt[eng] += delta
        return self.cnt[eng]

    def after(self, *engs):
        return [(e, self.cnt[e]) for e in engs if self.cnt[e] > 0]

    def barrier(self):
        snap = dict(self.cnt)
        for e in ENGS:
            waits = [(e2, snap[e2]) for e2 in ENGS if e2 != e and snap[e2] > 0]
            self.rec(e, lambda eng, sem: None, waits=waits, delta=0)

    def emit(self):
        nc = self.nc
        sems = {}
        ctxs = []
        for e in ENGS:
            cm = nc.semaphore(f"sem_{e}")
            sems[e] = cm.__enter__()
            ctxs.append(cm)
        blk = nc.Block()
        block = blk.__enter__()

        def mk(engname, engobj):
            seen = {}
            for (eng, fn, waits, delta) in self.ops:
                if eng != engname:
                    continue
                for (we, wv) in waits:
                    if we == engname:
                        continue
                    if seen.get(we, 0) >= wv:
                        continue
                    seen[we] = wv
                    engobj.wait_ge(sems[we], wv)
                fn(engobj, sems[eng])

        @block.tensor
        def _(e):
            mk("t", e)

        @block.vector
        def _(e):
            mk("v", e)

        @block.scalar
        def _(e):
            mk("s", e)

        @block.gpsimd
        def _(e):
            mk("g", e)

        @block.sync
        def _(e):
            mk("y", e)

        blk.__exit__(None, None, None)
        for cm in ctxs:
            cm.__exit__(None, None, None)


class Stage:

    def __init__(self, nc):
        self.nc = nc
        self.guards = []

    def sb(self, name, shape, dt_):
        g = self.nc.sbuf_tensor(name, shape, dt_)
        h = g.__enter__()
        self.guards.append(g)
        return h

    def close(self):
        for g in reversed(self.guards):
            g.__exit__(None, None, None)
        self.guards = []


def inc1(*insts_fn):
    def f(e, sem):
        inst = None
        for g in insts_fn:
            inst = g(e)
        inst.then_inc(sem, 1)
    return f



def host_layout(inputs):
    C = lambda a: np.ascontiguousarray(a)
    d = {}
    d["w_winT"] = C(np.asarray(inputs["L0_attn_W_in"]).T)
    d["w_woutT"] = C(np.asarray(inputs["L0_attn_W_out"]).T)
    d["w_wxT"] = C(np.asarray(inputs["L0_attn_W_x"]).T)
    d["c_convw"] = C(np.asarray(inputs["L0_attn_conv_w"])[:, 0, :]
                     .reshape(16, 128, 4).transpose(1, 0, 2).reshape(128, 64))
    d["c_convb"] = C(np.asarray(inputs["L0_attn_conv_b"]).reshape(16, 128).T)
    d["c_wdt"] = C(np.asarray(inputs["L0_attn_W_dt"])[:, 0].reshape(16, 128).T)
    d["c_bdt"] = C(np.asarray(inputs["L0_attn_b_dt"]).reshape(16, 128).T)
    d["c_alog"] = C(np.asarray(inputs["L0_attn_A_log"]).reshape(1, 16))
    d["c_dprm"] = C(np.asarray(inputs["L0_attn_D_param"]).reshape(16, 128).T)
    d["c_lnw"] = C(np.asarray(inputs["L0_attn_ln_w"]).reshape(16, 128).T)
    d["c_lnb"] = C(np.asarray(inputs["L0_attn_ln_b"]).reshape(16, 128).T)
    d["w_w1"] = C(np.asarray(inputs["L0_ffn_w1"]))
    d["c_b1"] = C(np.asarray(inputs["L0_ffn_b1"]).reshape(32, 128).T)
    d["w_w2"] = C(np.asarray(inputs["L0_ffn_w2"]))
    d["c_b2"] = C(np.asarray(inputs["L0_ffn_b2"]).reshape(8, 128).T)
    d["w_wqT"] = C(np.asarray(inputs["L1_attn_Wq"]).T)
    d["w_wkT"] = C(np.asarray(inputs["L1_attn_Wk"]).T)
    d["w_wvT"] = C(np.asarray(inputs["L1_attn_Wv"]).T)
    d["w_woT"] = C(np.asarray(inputs["L1_attn_Wo"]).T)
    d["w_weT"] = C(np.asarray(inputs["L1_attn_We"]).T)
    d["w_wgT"] = C(np.asarray(inputs["L1_ffn_w_gate"]).T)
    d["w_wuT"] = C(np.asarray(inputs["L1_ffn_w_up"]).T)
    d["w_wdT"] = C(np.asarray(inputs["L1_ffn_w_down"]).T)
    d["c_ident"] = np.eye(128, dtype=np.float32)
    x = np.asarray(inputs["L0_attn_x"])
    xs = [C(x[b]) for b in range(B)]
    return d, xs



def build_program(debug=()):
    debug = set(debug)
    nc = bass.Bass("TRN2", target_bir_lowering=False)
    S = Script(nc)
    c = {}

    x_d = nc.dram_tensor("x", [T, D], F32, kind="ExternalInput").ap()
    out_d = nc.dram_tensor("out", [T, D], F32, kind="ExternalOutput").ap()
    dr = {}
    for n, shape in [
        ("w_winT", [D, 2 * DI]), ("w_woutT", [DI, D]), ("w_wxT", [DI, 33]),
        ("w_w1", [D, FF]), ("w_w2", [FF, D]),
        ("w_wqT", [D, D]), ("w_wkT", [D, D]), ("w_wvT", [D, D]),
        ("w_woT", [D, D]), ("w_weT", [D, DE]),
        ("w_wgT", [D, FF]), ("w_wuT", [D, FF]), ("w_wdT", [FF, D]),
    ]:
        dr[n] = nc.dram_tensor(n, shape, F32, kind="ExternalInput").ap()
    for n, shape in [
        ("c_convw", [128, 64]), ("c_convb", [128, 16]), ("c_wdt", [128, 16]),
        ("c_bdt", [128, 16]), ("c_alog", [1, 16]), ("c_dprm", [128, 16]),
        ("c_lnw", [128, 16]), ("c_lnb", [128, 16]), ("c_b1", [128, 32]),
        ("c_b2", [128, 8]), ("c_ident", [128, 128]),
    ]:
        dr[n] = nc.dram_tensor(n, shape, F32, kind="ExternalInput").ap()

    sc = {}
    for n, shape, dt_ in [
        ("winT_h", [D, 2 * DI], BF16), ("winT_l", [D, 2 * DI], BF16),
        ("woutT_h", [DI, D], BF16), ("woutT_l", [DI, D], BF16),
        ("w1_h", [D, FF], BF16), ("w1_l", [D, FF], BF16),
        ("w2_h", [FF, D], F16), ("w2_l", [FF, D], F16),
        ("wqT_h", [D, D], BF16), ("wqT_l", [D, D], BF16),
        ("wkT_h", [D, D], BF16), ("wkT_l", [D, D], BF16),
        ("wvT_h", [D, D], BF16), ("wvT_l", [D, D], BF16),
        ("woT_h", [D, D], BF16), ("woT_l", [D, D], BF16),
        ("weT_h", [D, DE], BF16), ("weT_l", [D, DE], BF16),
        ("wgT_i", [D, FF], BF16), ("wuT_i", [D, FF], BF16),
        ("wdT_i", [FF, D], BF16),
    ]:
        sc[n] = nc.dram_tensor(
            "sc_" + n, shape, dt_,
            kind=("ExternalOutput" if _os2.environ.get("PREP_OUT") == "1"
                  else "Internal")).ap()

    sb = nc.alloc_sbuf_tensor
    cpool = sb("cpool", [128, 16], F32)
    ones_f = sb("ones_f", [128, 128], F32)
    ones_b = sb("ones_b", [128, 128], BF16)
    ident = sb("ident", [128, 128], F32)
    x_res = [sb(f"x_res{i}", [128, T], F32) for i in range(8)]
    xn_h = [sb(f"xn_h{i}", [128, T], BF16) for i in range(8)]
    xn_l = [sb(f"xn_l{i}", [128, T], BF16) for i in range(8)]
    smalls = {}
    for n in ("c_convw", "c_convb", "c_wdt", "c_bdt", "c_dprm", "c_lnw",
              "c_lnb", "c_b1", "c_b2"):
        smalls[n] = sb(n + "_t", [128, dr[n].shape[1]], F32)
    alog_t = sb("alog_t", [1, 16], F32)
    arep = sb("arep", [128, 16], F32)
    b1x = sb("b1x", [128, 32], F32)
    b2x = sb("b2x", [128, 8], F32)

    pb = [nc.alloc_psum_tensor(f"pb{i}", [128, 512], F32) for i in range(8)]

    wtmp = whib = wlob = None
    redt = sb("redt", [128, 24], F32)
    scale_g = sb("scale_g", [128, 2], F32)
    scale_u = sb("scale_u", [128, 2], F32)
    scale_d = sb("scale_d", [128, 2], F32)

    c.update(locals())

    CPOOL_VALS = {0: 0.25, 1: 1.0, 2: 1e-5, 3: 1e-4, 4: 10.0, 5: 0.0,
                  6: 0.125, 7: 256.0, 8: 0.5, 9: 2.0, 10: 0.1, 11: 0.31622776601683794, 12: 0.7}
    def init_consts(e, sem):
        for i, v in CPOOL_VALS.items():
            e.memset(cpool[:, i:i + 1], v)
        e.memset(ones_f[:], 1.0)
        inst = e.tensor_copy(ones_b[:], ones_f[:])
        inst.then_inc(sem, 1)
    S.rec("v", init_consts)
    c["cp"] = lambda i: cpool[:, i:i + 1]

    def load_smalls(e, sem):
        n_d = 0
        for n in smalls:
            e.dma_start(out=smalls[n][:], in_=dr[n][:, :]).then_inc(sem, 16)
            n_d += 1
        e.dma_start(out=alog_t[:], in_=dr["c_alog"][:, :]).then_inc(sem, 16)
        e.dma_start(out=ident[:], in_=dr["c_ident"][:, :]).then_inc(sem, 16)
    n_small_dma = (len(smalls) + 2)
    S.rec("y", load_smalls, delta=16 * n_small_dma)
    w_small = S.after("y")

    def derived(e, sem):
        e.tensor_scalar_mul(b1x[:], smalls["c_b1"][:], 256.0)
        e.tensor_scalar_mul(b2x[:], smalls["c_b2"][:], 256.0)
        inst = e.tensor_scalar_min(alog_t[:], alog_t[:], 5.0)
        inst.then_inc(sem, 1)
    S.rec("v", derived, waits=w_small + S.after("v"))
    def arep_exp(e, sem):
        e.activation(alog_t[:], alog_t[:], AF.Exp).then_inc(sem, 1)
    S.rec("s", arep_exp, waits=S.after("v"))
    def arep_neg(e, sem):
        e.tensor_scalar_mul(alog_t[:], alog_t[:], -1.0).then_inc(sem, 1)
    S.rec("v", arep_neg, waits=S.after("s"))
    def arep_bc(e, sem):
        e.matmul(pb[0][:, 0:16], ones_f[0:1, :], alog_t[0:1, :],
                 start=True, stop=True).then_inc(sem, 1)
    S.rec("t", arep_bc, waits=S.after("v"))
    def arep_cp(e, sem):
        e.tensor_copy(arep[:], pb[0][:, 0:16]).then_inc(sem, 1)
    S.rec("v", arep_cp, waits=S.after("t"))

    buf_free = [{"v": 0, "g": 0}, {"v": 0, "g": 0}]

    def blocks_of(rows, cols):
        out = []
        if cols >= 2048:
            cw = 2048
            for rc in range(rows // 128):
                for cb in range(cols // cw):
                    out.append((rc * 128, cb * cw, 1, cw))
        else:
            pack = min(2048 // cols, rows // 128)
            for rc in range(rows // (128 * pack)):
                out.append((rc * 128 * pack, 0, pack, cols))
        return out

    def _dma_block(e, sem, dst, src, r0, c0, pack, cw, cols):
        if pack > 1:
            inp = src[r0:r0 + 128 * pack, c0:c0 + cw].rearrange(
                "(j p) c -> p j c", p=128)
            outp = dst[:, 0:pack * cw].rearrange("p (j c) -> p j c", c=cw)
        else:
            inp = src[r0:r0 + 128, c0:c0 + cw]
            outp = dst[:, 0:cw]
        e.dma_start(out=outp, in_=inp).then_inc(sem, 16)

    def _dma_block_out(e, sem, src_tile, dst, r0, c0, pack, cw, cols):
        if pack > 1:
            outp = dst[r0:r0 + 128 * pack, c0:c0 + cw].rearrange(
                "(j p) c -> p j c", p=128)
            inp = src_tile[:, 0:pack * cw].rearrange("p (j c) -> p j c", c=cw)
        else:
            outp = dst[r0:r0 + 128, c0:c0 + cw]
            inp = src_tile[:, 0:cw]
        e.dma_start(out=outp, in_=inp).then_inc(sem, 16)

    def prep_pair(src, hi_sc, lo_sc, rows, cols, f16=False):
        part = _os2.environ.get("PREP_PART", "all")
        for bi, (r0, c0, pack, cw) in enumerate(blocks_of(rows, cols)):
            bu = bi % 2
            nel = pack * cw
            w = [(e2, v2) for (e2, v2) in [("v", buf_free[bu]["v"]),
                                           ("g", buf_free[bu]["g"])] if v2]
            S.rec("y", lambda e, sem, a=(r0, c0, pack, cw), bu=bu:
                  _dma_block(e, sem, wtmp[bu][:], dr[src], *a, cols),
                  waits=w, delta=16)
            wy = S.after("y")
            hview = (whib[bu][:].bitcast(F16) if f16 else whib[bu][:])
            lview = (wlob[bu][:].bitcast(F16) if f16 else wlob[bu][:])
            if part == "load":
                buf_free[bu]["v"] = 0
                continue
            S.rec("s", inc1(lambda e, bu=bu, nel=nel, hview=hview:
                            e.activation(hview[:, 0:nel], wtmp[bu][:, 0:nel],
                                         AF.Identity)), waits=wy)
            if part == "hi":
                continue
            S.rec("v", inc1(lambda e, bu=bu, nel=nel, hview=hview, lview=lview:
                            e.tensor_sub(lview[:, 0:nel], wtmp[bu][:, 0:nel],
                                         hview[:, 0:nel])),
                  waits=S.after("s"))
            buf_free[bu]["v"] = S.cnt["v"]
            wv = S.after("v", "s")
            if part == "lo":
                continue
            def stboth(e, sem, a=(r0, c0, pack, cw), bu=bu, hview=hview,
                       lview=lview, hi_sc=hi_sc, lo_sc=lo_sc):
                _dma_block_out(e, sem, hview, sc[hi_sc], *a, cols)
                _dma_block_out(e, sem, lview, sc[lo_sc], *a, cols)
            S.rec("g", stboth, waits=wv, delta=32)
            buf_free[bu]["g"] = S.cnt["g"]

    def prep_int(src, int_sc, scale_t, rows, cols):
        blks = blocks_of(rows, cols)
        nb = len(blks)
        assert nb <= 20
        for bi, (r0, c0, pack, cw) in enumerate(blks):
            bu = bi % 2
            nel = pack * cw
            w = [(e2, v2) for (e2, v2) in [("v", buf_free[bu]["v"]),
                                           ("g", buf_free[bu]["g"])] if v2]
            S.rec("y", lambda e, sem, a=(r0, c0, pack, cw), bu=bu:
                  _dma_block(e, sem, wtmp[bu][:], dr[src], *a, cols),
                  waits=w, delta=16)
            wy = S.after("y")
            S.rec("v", inc1(lambda e, bu=bu, nel=nel, bi=bi: e.reduce_sum(
                redt[:, 4 + bi:5 + bi], wtmp[bu][:, 0:nel], axis=AX.X,
                apply_absolute_value=True)), waits=wy)
            buf_free[bu]["v"] = S.cnt["v"]
        S.rec("s", inc1(lambda e: e.activation(
            redt[:, 4:4 + nb], redt[:, 4:4 + nb], AF.Identity,
            accum_out=redt[:, 0:1])), waits=S.after("v"))
        S.rec("t", inc1(lambda e: e.matmul(
            pb[6][:, 0:1], ones_f[:], redt[:, 0:1], start=True, stop=True)),
            waits=S.after("s"))
        S.rec("s", inc1(lambda e: e.activation(
            redt[:, 2:3], pb[6][:, 0:1], AF.Identity,
            scale=1.0 / (rows * cols))), waits=S.after("t"))
        S.rec("v", inc1(lambda e, scale_t=scale_t: e.tensor_scalar_max(
            scale_t[:, 1:2], redt[:, 2:3], 1e-5)), waits=S.after("s"))
        S.rec("s", inc1(lambda e, scale_t=scale_t: e.activation(
            redt[:, 3:4], scale_t[:, 1:2], AF.Identity)), waits=S.after("v"))
        S.rec("v", inc1(lambda e, scale_t=scale_t: e.reciprocal(
            scale_t[:, 0:1], redt[:, 3:4])), waits=S.after("s"))
        wscale = S.after("v")
        for bi, (r0, c0, pack, cw) in enumerate(blks):
            bu = bi % 2
            nel = pack * cw
            w = [(e2, v2) for (e2, v2) in [("v", buf_free[bu]["v"]),
                                           ("g", buf_free[bu]["g"])] if v2]
            S.rec("y", lambda e, sem, a=(r0, c0, pack, cw), bu=bu:
                  _dma_block(e, sem, wtmp[bu][:], dr[src], *a, cols),
                  waits=w, delta=16)
            wy = S.after("y")
            S.rec("s", inc1(lambda e, bu=bu, nel=nel, scale_t=scale_t:
                            e.activation(wtmp[bu][:, 0:nel], wtmp[bu][:, 0:nel],
                                         AF.Identity, scale=scale_t[:, 0:1])),
                  waits=wy + (wscale if bi < 2 else []) + S.after("v"))
            def quant(e, sem, bu=bu, nel=nel):
                iview = wtmp[bu][:].bitcast(I32)
                e.tensor_copy(iview[:, 0:nel], wtmp[bu][:, 0:nel])
                e.tensor_copy(whib[bu][:, 0:nel], iview[:, 0:nel])
                e.tensor_scalar(whib[bu][:, 0:nel], whib[bu][:, 0:nel],
                                -1.0, 1.0, ALU.max, ALU.min).then_inc(sem, 1)
            S.rec("v", quant, waits=S.after("s"))
            buf_free[bu]["v"] = S.cnt["v"]
            def stint(e, sem, a=(r0, c0, pack, cw), bu=bu, int_sc=int_sc):
                _dma_block_out(e, sem, whib[bu][:], sc[int_sc], *a, cols)
            S.rec("g", stint, waits=S.after("v"), delta=16)
            buf_free[bu]["g"] = S.cnt["g"]

    def free_prep_bufs():
        c["prep_stage"].close()

    c["free_prep_bufs"] = free_prep_bufs
    c["prep_pair"] = prep_pair
    c["prep_int"] = prep_int
    w1h_s = [sb(f"w1h_s{i}", [128, 1024], BF16) for i in range(2)]
    w1l_s = [sb(f"w1l_s{i}", [128, 1024], BF16) for i in range(2)]
    front_stage = Stage(nc)
    lnt = {n: front_stage.sb("ln_" + n, [128, 512], F32)
           for n in ("mu", "ex2", "var", "rstd")}
    sqt = [front_stage.sb(f"ln_sq{i}", [128, 512], F32) for i in range(2)]
    xtm_stage = Stage(nc)
    xtm = xtm_stage.sb("xtm", [128, 1024], F32)

    def ln_feature_major(src_tiles, nchunks, dst_h, dst_l, wait_src):
        for nt in range(4):
            sl = slice(512 * nt, 512 * (nt + 1))
            w0 = wait_src if nt == 0 else []
            def fsum(e, sem, sl=sl):
                for cc in range(nchunks):
                    inst = e.matmul(pb[6][:], ones_f[:], src_tiles[cc][:, sl],
                                    start=(cc == 0), stop=(cc == nchunks - 1))
                inst.then_inc(sem, 1)
            S.rec("t", fsum, waits=w0 + S.after("v"))
            for cc in range(nchunks):
                bu = cc % 2
                S.rec("s", inc1(lambda e, cc=cc, sl=sl, bu=bu: e.activation(
                    sqt[bu][:], src_tiles[cc][:, sl], AF.Square)),
                    waits=w0 + S.after("t"))
                S.rec("t", inc1(lambda e, cc=cc, bu=bu: e.matmul(
                    pb[7][:], ones_f[:], sqt[bu][:],
                    start=(cc == 0), stop=(cc == nchunks - 1))),
                    waits=S.after("s"))
            wt = S.after("t")
            def mkstats(e, sem, n=nchunks):
                e.tensor_scalar_mul(lnt["mu"][:], pb[6][:], 1.0 / (128 * n))
                e.tensor_scalar_mul(lnt["ex2"][:], pb[7][:], 1.0 / (128 * n))
                e.tensor_tensor(lnt["var"][:], lnt["mu"][:], lnt["mu"][:], ALU.mult)
                e.tensor_sub(lnt["var"][:], lnt["ex2"][:], lnt["var"][:]).then_inc(sem, 1)
            S.rec("v", mkstats, waits=wt)
            S.rec("s", inc1(lambda e: e.activation(
                lnt["rstd"][:], lnt["var"][:], AF.Sqrt, bias=cpool[:, 2:3])),
                waits=S.after("v"))
            S.rec("v", inc1(lambda e: e.reciprocal(lnt["rstd"][:], lnt["rstd"][:])),
                  waits=S.after("s"))
            for cc in range(nchunks):
                def app(e, sem, cc=cc, sl=sl):
                    e.tensor_sub(lnt["var"][:], src_tiles[cc][:, sl], lnt["mu"][:])
                    e.tensor_tensor(lnt["var"][:], lnt["var"][:], lnt["rstd"][:],
                                    ALU.mult)
                    e.tensor_copy(dst_h[cc][:, sl], lnt["var"][:])
                    e.tensor_sub(dst_l[cc][:, sl], lnt["var"][:],
                                 dst_h[cc][:, sl]).then_inc(sem, 1)
                S.rec("v", app)

    def load_x():
        for i in range(16):
            wv = S.after("t") if i else []
            S.rec("y", lambda e, sem, i=i: e.dma_start(
                out=xtm[:], in_=x_d[128 * i:128 * (i + 1), :]).then_inc(sem, 16),
                waits=wv, delta=16)
            wy = S.after("y")
            def tr(e, sem):
                for cc in range(8):
                    e.transpose(pb[cc][:, 0:128],
                                xtm[:, 128 * cc:128 * (cc + 1)],
                                ident[:]).then_inc(sem, 1)
            S.rec("t", tr, waits=wy + S.after("v"), delta=8)
            wt = S.after("t")
            def cpy(e, sem, i=i):
                for cc in range(8):
                    inst = e.tensor_copy(x_res[cc][:, 128 * i:128 * (i + 1)],
                                         pb[cc][:, 0:128])
                inst.then_inc(sem, 1)
            S.rec("v", cpy, waits=wt)

    c["load_x"] = load_x
    c["ln_feature_major"] = ln_feature_major


    def ffn0():
        st_ = Stage(nc)
        hq = [st_.sb(f"hq{i}", [128, 512], F16) for i in range(32)]
        w2h_s = [st_.sb(f"w2h_s{i}", [128, 512], F16) for i in range(2)]
        w2l_s = [st_.sb(f"w2l_s{i}", [128, 512], F16) for i in range(2)]
        pw = {n: st_.sb("pw_" + n, [128, 512], F32)
              for n in ("hu", "kf", "t1", "glo", "ghi")}
        pw["jf"] = st_.sb("pw_jf", [128, 512], BF16)
        pwi = st_.sb("pw_i32", [128, 512], I32)
        c["ffn0_stage"] = st_
        wprep = S.after("g")
        for nt in range(4):
            sl = slice(512 * nt, 512 * (nt + 1))
            evac_done = {}
            ldw1_done = {}
            pwlb_done = {}
            nt_t = S.after("t")
            for oc in range(32):
                bu = oc % 2
                if oc >= 2:
                    w = [("t", ldw1_done[oc - 2])]
                else:
                    w = (list(wprep) if nt == 0 else []) + list(nt_t)
                S.rec("y", lambda e, sem, oc=oc, bu=bu: (
                    e.dma_start(out=w1h_s[bu][:].rearrange("p (k c) -> p k c", c=128),
                                in_=sc["w1_h"][:, 128 * oc:128 * (oc + 1)]
                                .rearrange("(k p) c -> p k c", p=128)).then_inc(sem, 16),
                    e.dma_start(out=w1l_s[bu][:].rearrange("p (k c) -> p k c", c=128),
                                in_=sc["w1_l"][:, 128 * oc:128 * (oc + 1)]
                                .rearrange("(k p) c -> p k c", p=128)).then_inc(sem, 16)),
                    waits=w, delta=32)
                wy = S.after("y")
                pbH = pb[4 + bu]
                wps = [("s", evac_done[oc - 2])] if oc >= 2 else []
                def mm(e, sem, bu=bu, pbH=pbH, sl=sl):
                    for k in range(8):
                        ck = slice(128 * k, 128 * (k + 1))
                        e.matmul(pbH[:], w1h_s[bu][:, ck], xn_h[k][:, sl],
                                 start=(k == 0), stop=False)
                        e.matmul(pbH[:], w1h_s[bu][:, ck], xn_l[k][:, sl],
                                 start=False, stop=False)
                        inst = e.matmul(pbH[:], w1l_s[bu][:, ck], xn_h[k][:, sl],
                                        start=False, stop=(k == 7))
                    inst.then_inc(sem, 1)
                S.rec("t", mm, waits=wy + wps)
                ldw1_done[oc] = S.cnt["t"]
                wt = S.after("t")
                whu = ([("v", pwlb_done[oc - 1])] if oc >= 1
                       else S.after("v"))
                S.rec("s", inc1(lambda e, pbH=pbH, oc=oc: e.activation(
                    pw["hu"][:], pbH[:], AF.Identity, bias=b1x[:, oc:oc + 1],
                    scale=cpool[:, 7:8])), waits=wt + whu)
                evac_done[oc] = S.cnt["s"]
                ws = S.after("s")
                def pwl_a(e, sem):
                    e.tensor_copy(pwi[:], pw["hu"][:])
                    e.tensor_copy(pw["kf"][:], pwi[:])
                    e.tensor_scalar(pw["t1"][:], pw["kf"][:], 1.0 / 64,
                                    -0.49609375, ALU.mult, ALU.add)
                    e.tensor_copy(pwi[:], pw["t1"][:])
                    e.tensor_copy(pw["jf"][:], pwi[:])
                    e.tensor_scalar(pw["jf"][:], pw["jf"][:], -16.0, 15.0,
                                    ALU.max, ALU.min)
                    e.tensor_scalar(pw["t1"][:], pw["kf"][:], 1.0 / 64,
                                    0.0, ALU.mult, ALU.add)
                    e.tensor_sub(pw["t1"][:], pw["t1"][:],
                                 pw["jf"][:]).then_inc(sem, 1)
                S.rec("v", pwl_a, waits=ws)
                wv = S.after("v")
                S.rec("s", inc1(
                    lambda e: e.activation(pw["glo"][:], pw["jf"][:], AF.Gelu,
                                           scale=cpool[:, 0:1]),
                    lambda e: e.activation(pw["ghi"][:], pw["jf"][:], AF.Gelu,
                                           scale=cpool[:, 0:1], bias=cpool[:, 0:1])),
                    waits=wv)
                ws2 = S.after("s")
                def pwl_b(e, sem, oc=oc):
                    e.tensor_sub(pw["hu"][:], pw["ghi"][:], pw["glo"][:])
                    e.tensor_tensor(pw["hu"][:], pw["hu"][:], pw["t1"][:], ALU.mult)
                    e.tensor_add(pw["hu"][:], pw["hu"][:], pw["glo"][:])
                    e.tensor_single_scalar(pw["ghi"][:], pw["kf"][:], 1024.5,
                                           ALU.is_ge)
                    e.scalar_tensor_tensor(pw["t1"][:], pw["kf"][:], 1.0 / 256,
                                           pw["hu"][:], ALU.mult, ALU.subtract)
                    e.tensor_tensor(pw["t1"][:], pw["t1"][:], pw["ghi"][:], ALU.mult)
                    e.tensor_add(pw["hu"][:], pw["hu"][:], pw["t1"][:])
                    e.tensor_single_scalar(pw["ghi"][:], pw["kf"][:], -1024.5,
                                           ALU.is_le)
                    e.tensor_scalar(pw["ghi"][:], pw["ghi"][:], -1.0, 1.0,
                                    ALU.mult, ALU.add)
                    e.tensor_tensor(pw["hu"][:], pw["hu"][:], pw["ghi"][:], ALU.mult)
                    e.tensor_scalar_mul(pw["hu"][:], pw["hu"][:], 256.0)
                    e.tensor_copy(pwi[:], pw["hu"][:])
                    e.tensor_copy(hq[oc][:], pwi[:]).then_inc(sem, 1)
                S.rec("v", pwl_b, waits=ws2)
                pwlb_done[oc] = S.cnt["v"]
            for half in range(2):
                wall = S.after("v", "s")
                prev_t = S.after("t")
                ld_done = {}
                for oc in range(32):
                    bu = oc % 2
                    w = [("t", ld_done[oc - 2])] if oc >= 2 else list(prev_t)
                    S.rec("y", lambda e, sem, oc=oc, bu=bu, half=half: (
                        e.dma_start(out=w2h_s[bu][:],
                                    in_=sc["w2_h"][128 * oc:128 * (oc + 1),
                                                   512 * half:512 * (half + 1)])
                        .then_inc(sem, 16),
                        e.dma_start(out=w2l_s[bu][:],
                                    in_=sc["w2_l"][128 * oc:128 * (oc + 1),
                                                   512 * half:512 * (half + 1)])
                        .then_inc(sem, 16)), waits=w, delta=32)
                    wy = S.after("y")
                    def mm2(e, sem, oc=oc, bu=bu):
                        for d4 in range(4):
                            ck = slice(128 * d4, 128 * (d4 + 1))
                            e.matmul(pb[d4][:], w2h_s[bu][:, ck], hq[oc][:],
                                     start=(oc == 0), stop=False)
                            inst = e.matmul(pb[d4][:], w2l_s[bu][:, ck], hq[oc][:],
                                            start=False, stop=(oc == 31))
                        inst.then_inc(sem, 1)
                    S.rec("t", mm2, waits=wy + (wall if oc == 0 else []))
                    ld_done[oc] = S.cnt["t"]
                wt = S.after("t")
                for d4 in range(4):
                    cc = 4 * half + d4
                    S.rec("s", inc1(lambda e, d4=d4, cc=cc: e.activation(
                        pw["hu"][:], pb[d4][:], AF.Identity, bias=b2x[:, cc:cc + 1])),
                        waits=wt + S.after("v"))
                    def fin(e, sem, cc=cc, sl=sl):
                        e.tensor_copy(pwi[:], pw["hu"][:])
                        e.tensor_copy(pw["kf"][:], pwi[:])
                        e.scalar_tensor_tensor(x_res[cc][:, sl], pw["kf"][:],
                                               1.0 / 256, x_res[cc][:, sl],
                                               ALU.mult, ALU.add).then_inc(sem, 1)
                    S.rec("v", fin, waits=S.after("s"))

    c["ffn0"] = ffn0

    def free_ffn0_tiles():
        c["ffn0_stage"].close()

    def mamba():
        if "mtap" in debug:
            c["dbg_mxi"] = nc.dram_tensor("dbg_mxi", [2048, 2048], F32,
                                          kind="ExternalOutput").ap()
            c["dbg_mxd"] = nc.dram_tensor("dbg_mxd", [33, 2048], F32,
                                          kind="ExternalOutput").ap()
            c["dbg_my"] = nc.dram_tensor("dbg_my", [2048, 2048], F32,
                                         kind="ExternalOutput").ap()
            c["dbg_mhc"] = nc.dram_tensor("dbg_mhc", [128, 256], F32,
                                          kind="ExternalOutput").ap()
            c["dbg_mgh"] = nc.dram_tensor("dbg_mgh", [2048, 2048], BF16,
                                          kind="ExternalOutput").ap()
            c["dbg_mgl"] = nc.dram_tensor("dbg_mgl", [2048, 2048], BF16,
                                          kind="ExternalOutput").ap()
        wprep = S.after("g")
        TC = 256
        NTC = T // TC
        st_ = Stage(nc)
        wxT_t = st_.sb("wxT_t", [128, 16 * 33], F32)
        def ldwx(e, sem):
            for k in range(16):
                e.dma_start(out=wxT_t[:, 33 * k:33 * (k + 1)],
                            in_=dr["w_wxT"][128 * k:128 * (k + 1), :])\
                    .then_inc(sem, 16)
        S.rec("y", ldwx, delta=16 * 16)
        xi_t = st_.sb("xi_t", [128, 16 * TC], F32)
        xiraw = st_.sb("xiraw", [128, TC + 3], F32)
        carry3 = st_.sb("carry3", [128, 3 * 16], F32)
        dtq = st_.sb("dtq", [128, 4 * TC], F32)
        uq = st_.sb("uq", [128, 4 * TC], F32)
        dA_t = st_.sb("dA_t", [128, 4 * TC], F32)
        dbu = st_.sb("dbu", [128, 4 * TC], F32)
        h_q = dbu
        yacc = st_.sb("yacc", [128, 16 * TC], F32)
        xd_s = st_.sb("xd_s", [33, TC], F32)
        hcarry = st_.sb("hcarry", [128, 256], F32)
        gate_h = st_.sb("gate_h", [128, 4 * TC], BF16)
        gate_l = st_.sb("gate_l", [128, 4 * TC], BF16)


        S.rec("v", inc1(lambda e: (e.memset(carry3[:], 0.0),
                                   e.memset(xiraw[:, 0:3], 0.0),
                                   e.memset(hcarry[:], 0.0))[-1]))

        wstat_done = [0, 0]
        evac_state = {"pb0": 0, "pb1": 0}

        for tc in range(NTC):
            tsl = slice(TC * tc, TC * (tc + 1))
            S.rec("v", inc1(lambda e: e.tensor_copy(xiraw[:, 0:3],
                                                    carry3[:, 0:3])))
            for db in range(16):
                bu = db % 2
                w = [("t", wstat_done[bu])] if wstat_done[bu] else list(wprep)
                S.rec("y", lambda e, sem, db=db, bu=bu: (
                    e.dma_start(out=w1h_s[bu][:].rearrange("p (k c) -> p k c", c=128),
                                in_=sc["winT_h"][:, 128 * db:128 * (db + 1)]
                                .rearrange("(k p) c -> p k c", p=128)).then_inc(sem, 16),
                    e.dma_start(out=w1l_s[bu][:].rearrange("p (k c) -> p k c", c=128),
                                in_=sc["winT_l"][:, 128 * db:128 * (db + 1)]
                                .rearrange("(k p) c -> p k c", p=128)).then_inc(sem, 16)),
                    waits=w, delta=32)
                wy = S.after("y")
                pbu = pb[bu]
                wps = [("s", evac_state[f"pb{bu}"])] if evac_state[f"pb{bu}"] else []
                def mmxi(e, sem, bu=bu, pbu=pbu, tsl=tsl):
                    for k in range(8):
                        ck = slice(128 * k, 128 * (k + 1))
                        e.matmul(pbu[:, 0:TC], w1h_s[bu][:, ck], xn_h[k][:, tsl],
                                 start=(k == 0), stop=False)
                        e.matmul(pbu[:, 0:TC], w1h_s[bu][:, ck], xn_l[k][:, tsl],
                                 start=False, stop=False)
                        inst = e.matmul(pbu[:, 0:TC], w1l_s[bu][:, ck],
                                        xn_h[k][:, tsl], start=False, stop=(k == 7))
                    inst.then_inc(sem, 1)
                S.rec("t", mmxi, waits=wy + wps)
                wstat_done[bu] = S.cnt["t"]
                wt = S.after("t")
                S.rec("s", inc1(lambda e, pbu=pbu: e.activation(
                    xiraw[:, 3:3 + TC], pbu[:, 0:TC], AF.Identity)),
                    waits=wt + S.after("v"))
                evac_state[f"pb{bu}"] = S.cnt["s"]
                ws = S.after("s")
                def conv(e, sem, db=db):
                    for k in range(4):
                        if k == 0:
                            e.tensor_single_scalar(
                                dtq[:, 0:TC], xiraw[:, 0:TC],
                                smalls["c_convw"][:, 4 * db:4 * db + 1], ALU.mult)
                        else:
                            e.scalar_tensor_tensor(
                                dtq[:, 0:TC], xiraw[:, k:k + TC],
                                smalls["c_convw"][:, 4 * db + k:4 * db + k + 1],
                                dtq[:, 0:TC], ALU.mult, ALU.add)
                    e.tensor_copy(carry3[:, 3 * db:3 * (db + 1)],
                                  xiraw[:, TC:TC + 3])
                    ndb = (db + 1) % 16
                    e.tensor_copy(xiraw[:, 0:3],
                                  carry3[:, 3 * ndb:3 * (ndb + 1)])\
                        .then_inc(sem, 1)
                S.rec("v", conv, waits=ws)
                S.rec("s", inc1(lambda e, db=db: e.activation(
                    xi_t[:, TC * db:TC * (db + 1)], dtq[:, 0:TC], AF.Silu,
                    bias=smalls["c_convb"][:, db:db + 1])), waits=S.after("v"))
            if "mtap" in debug:
                def dmp_xi(e, sem, tc=tc):
                    for db in range(16):
                        e.dma_start(
                            out=c["dbg_mxi"][128 * db:128 * (db + 1),
                                             TC * tc:TC * (tc + 1)],
                            in_=xi_t[:, TC * db:TC * (db + 1)]).then_inc(sem, 16)
                S.rec("g", dmp_xi, waits=S.after("s"), delta=16 * 16)
                S.rec("v", lambda e, sem: None, waits=S.after("g"), delta=0)
            wxd = S.after("s")
            def mmxd(e, sem):
                for k in range(16):
                    inst = e.matmul(pb[2][0:33, 0:TC], wxT_t[:, 33 * k:33 * (k + 1)],
                                    xi_t[:, TC * k:TC * (k + 1)],
                                    start=(k == 0), stop=(k == 15))
                inst.then_inc(sem, 1)
            S.rec("t", mmxd, waits=wxd + S.after("v"))
            S.rec("v", inc1(lambda e: e.tensor_copy(xd_s[:], pb[2][0:33, 0:TC])),
                  waits=S.after("t"))
            wxds = S.after("v")
            if "mtap" in debug:
                S.rec("g", lambda e, sem, tc=tc: e.dma_start(
                    out=c["dbg_mxd"][:, TC * tc:TC * (tc + 1)],
                    in_=xd_s[:]).then_inc(sem, 16), waits=wxds, delta=16)
                S.rec("t", lambda e, sem: None, waits=S.after("g"), delta=0)
            def dtbc(e, sem):
                e.matmul(pb[2][:, 0:TC],
                         ident[0:33, 0:1].broadcast_to([33, 128]),
                         xd_s[:, :], start=True, stop=True).then_inc(sem, 1)
            S.rec("t", dtbc, waits=wxds)
            wdtbc = S.after("t")
            for q in range(4):
                for i in range(4):
                    db = 4 * q + i
                    S.rec("s", inc1(
                        lambda e, db=db, i=i: e.activation(
                            dtq[:, TC * i:TC * (i + 1)], pb[2][:, 0:TC], AF.Exp,
                            scale=smalls["c_wdt"][:, db:db + 1],
                            bias=smalls["c_bdt"][:, db:db + 1]),
                        lambda e, i=i: e.activation(
                            dtq[:, TC * i:TC * (i + 1)], dtq[:, TC * i:TC * (i + 1)],
                            AF.Ln, bias=cpool[:, 1:2])),
                        waits=(wdtbc if (q == 0 and i == 0) else []) + S.after("v"))
                S.rec("v", inc1(lambda e: e.tensor_scalar(
                    dtq[:], dtq[:], 1e-4, 10.0, ALU.max, ALU.min)),
                    waits=S.after("s"))
                S.rec("v", inc1(lambda e, q=q: e.tensor_tensor(
                    uq[:], dtq[:], xi_t[:, 4 * TC * q:4 * TC * (q + 1)], ALU.mult)))
                wu = S.after("v")
                for s_ in range(16):
                    S.rec("s", inc1(lambda e, s_=s_: e.activation(
                        dA_t[:], dtq[:], AF.Exp, scale=arep[:, s_:s_ + 1])),
                        waits=(wu if s_ == 0 else []) + S.after("v"))
                    def brep(e, sem, s_=s_):
                        e.matmul(pb[3][:, 0:TC],
                                 ident[0:33, 1 + s_:2 + s_]
                                 .broadcast_to([33, 128]),
                                 xd_s[:, :], start=True,
                                 stop=True).then_inc(sem, 1)
                        e.matmul(pb[4][:, 0:TC],
                                 ident[0:33, 17 + s_:18 + s_]
                                 .broadcast_to([33, 128]),
                                 xd_s[:, :], start=True,
                                 stop=True).then_inc(sem, 1)
                    S.rec("t", brep, waits=S.after("v"), delta=2)
                    wb = S.after("t", "s")
                    def dbustep(e, sem):
                        e.tensor_tensor(
                            dbu[:].rearrange("p (a n) -> p a n", n=TC),
                            uq[:].rearrange("p (a n) -> p a n", n=TC),
                            pb[3][:, 0:TC].rearrange("p (a n) -> p a n", a=1)
                            .broadcast_to([128, 4, TC]), ALU.mult)
                        e.tensor_scalar(dbu[:], dbu[:], -10.0, 10.0,
                                        ALU.max, ALU.min).then_inc(sem, 1)
                    S.rec("v", dbustep, waits=wb)
                    def scans(e, sem, q=q, s_=s_):
                        for i in range(4):
                            col = (4 * q + i) * 16 + s_
                            inst = e.tensor_tensor_scan(
                                h_q[:, TC * i:TC * (i + 1)],
                                dA_t[:, TC * i:TC * (i + 1)],
                                dbu[:, TC * i:TC * (i + 1)],
                                hcarry[:, col:col + 1], ALU.mult, ALU.add)
                        inst.then_inc(sem, 1)
                    S.rec("v", scans)
                    def yaccf(e, sem, q=q, s_=s_):
                        if s_ == 0:
                            e.tensor_tensor(
                                yacc[:, 4 * TC * q:4 * TC * (q + 1)]
                                .rearrange("p (a n) -> p a n", n=TC),
                                h_q[:].rearrange("p (a n) -> p a n", n=TC),
                                pb[4][:, 0:TC].rearrange("p (a n) -> p a n", a=1)
                                .broadcast_to([128, 4, TC]),
                                ALU.mult)
                        else:
                            e.tensor_tensor(
                                dA_t[:].rearrange("p (a n) -> p a n", n=TC),
                                h_q[:].rearrange("p (a n) -> p a n", n=TC),
                                pb[4][:, 0:TC].rearrange("p (a n) -> p a n", a=1)
                                .broadcast_to([128, 4, TC]), ALU.mult)
                            e.tensor_add(yacc[:, 4 * TC * q:4 * TC * (q + 1)],
                                         yacc[:, 4 * TC * q:4 * TC * (q + 1)],
                                         dA_t[:])
                        e.tensor_copy(
                            hcarry[:].rearrange("p (d s) -> p d s", s=16)
                            [:, 4 * q:4 * q + 4, s_],
                            h_q[:].rearrange("p (a n) -> p a n", n=TC)[:, :, TC - 1]
                        ).then_inc(sem, 1)
                    S.rec("v", yaccf)
            if "mtap" in debug and tc == 0:
                S.rec("g", lambda e, sem: e.dma_start(
                    out=c["dbg_mhc"][:, :], in_=hcarry[:]).then_inc(sem, 16),
                    waits=S.after("v"), delta=16)
                S.rec("v", lambda e, sem: None, waits=S.after("g"), delta=0)
            if "mtap" in debug:
                def dmp_y(e, sem, tc=tc):
                    for db in range(16):
                        e.dma_start(
                            out=c["dbg_my"][128 * db:128 * (db + 1),
                                            TC * tc:TC * (tc + 1)],
                            in_=yacc[:, TC * db:TC * (db + 1)]).then_inc(sem, 16)
                S.rec("g", dmp_y, waits=S.after("v"), delta=16 * 16)
                S.rec("v", lambda e, sem: None, waits=S.after("g"), delta=0)
            wsc = S.after("v")
            def ylnsum(e, sem):
                for k in range(16):
                    inst = e.matmul(pb[5][:, 0:TC], ones_f[:],
                                    yacc[:, TC * k:TC * (k + 1)],
                                    start=(k == 0), stop=(k == 15))
                inst.then_inc(sem, 1)
            S.rec("t", ylnsum, waits=wsc)
            for k in range(16):
                bu2 = k % 2
                S.rec("s", inc1(lambda e, k=k, bu2=bu2: e.activation(
                    sqt[bu2][:, 0:TC], yacc[:, TC * k:TC * (k + 1)], AF.Square)),
                    waits=wsc + S.after("t"))
                S.rec("t", inc1(lambda e, k=k, bu2=bu2: e.matmul(
                    pb[6][:, 0:TC], ones_f[:], sqt[bu2][:, 0:TC],
                    start=(k == 0), stop=(k == 15))), waits=S.after("s"))
            wt = S.after("t")
            def ymk(e, sem):
                e.tensor_scalar_mul(lnt["mu"][:, 0:TC], pb[5][:, 0:TC], 1.0 / DI)
                e.tensor_scalar_mul(lnt["ex2"][:, 0:TC], pb[6][:, 0:TC], 1.0 / DI)
                e.tensor_tensor(lnt["var"][:, 0:TC], lnt["mu"][:, 0:TC],
                                lnt["mu"][:, 0:TC], ALU.mult)
                e.tensor_sub(lnt["var"][:, 0:TC], lnt["ex2"][:, 0:TC],
                             lnt["var"][:, 0:TC]).then_inc(sem, 1)
            S.rec("v", ymk, waits=wt)
            S.rec("s", inc1(lambda e: e.activation(
                lnt["rstd"][:, 0:TC], lnt["var"][:, 0:TC], AF.Sqrt,
                bias=cpool[:, 2:3])), waits=S.after("v"))
            S.rec("v", inc1(lambda e: e.reciprocal(lnt["rstd"][:, 0:TC],
                                                   lnt["rstd"][:, 0:TC])),
                  waits=S.after("s"))
            wstats = S.after("v")
            gate_done = 0
            for hg in range(4):
                for dbl in range(4):
                    db = 4 * hg + dbl
                    bu = db % 2
                    w = [("t", wstat_done[bu])] if wstat_done[bu] else list(wprep)
                    S.rec("y", lambda e, sem, db=db, bu=bu: (
                        e.dma_start(out=w1h_s[bu][:]
                                    .rearrange("p (k c) -> p k c", c=128),
                                    in_=sc["winT_h"][:, 2048 + 128 * db:
                                                     2048 + 128 * (db + 1)]
                                    .rearrange("(k p) c -> p k c", p=128))
                        .then_inc(sem, 16),
                        e.dma_start(out=w1l_s[bu][:]
                                    .rearrange("p (k c) -> p k c", c=128),
                                    in_=sc["winT_l"][:, 2048 + 128 * db:
                                                     2048 + 128 * (db + 1)]
                                    .rearrange("(k p) c -> p k c", p=128))
                        .then_inc(sem, 16)), waits=w, delta=32)
                    wy = S.after("y")
                    pbu = pb[bu]
                    wps = ([("s", evac_state[f"pb{bu}"])]
                           if evac_state[f"pb{bu}"] else [])
                    def mmz(e, sem, bu=bu, pbu=pbu, tsl=tsl):
                        for k in range(8):
                            ck = slice(128 * k, 128 * (k + 1))
                            e.matmul(pbu[:, 0:TC], w1h_s[bu][:, ck],
                                     xn_h[k][:, tsl], start=(k == 0), stop=False)
                            e.matmul(pbu[:, 0:TC], w1h_s[bu][:, ck],
                                     xn_l[k][:, tsl], start=False, stop=False)
                            inst = e.matmul(pbu[:, 0:TC], w1l_s[bu][:, ck],
                                            xn_h[k][:, tsl], start=False,
                                            stop=(k == 7))
                        inst.then_inc(sem, 1)
                    S.rec("t", mmz, waits=wy + wps +
                          (wstats if (hg == 0 and dbl == 0) else []))
                    wstat_done[bu] = S.cnt["t"]
                    wg = [("v", gate_done)] if gate_done else []
                    S.rec("s", inc1(lambda e, pbu=pbu: e.activation(
                        xiraw[:, 0:TC], pbu[:, 0:TC], AF.Silu)),
                        waits=S.after("t") + wg)
                    evac_state[f"pb{bu}"] = S.cnt["s"]
                    def lnapply(e, sem, db=db):
                        e.tensor_sub(dtq[:, 0:TC], yacc[:, TC * db:TC * (db + 1)],
                                     lnt["mu"][:, 0:TC])
                        e.tensor_tensor(dtq[:, 0:TC], dtq[:, 0:TC], lnt["rstd"][:, 0:TC],
                                        ALU.mult).then_inc(sem, 1)
                    S.rec("v", lnapply, waits=S.after("s"))
                    S.rec("s", inc1(lambda e, db=db: e.activation(
                        dtq[:, 0:TC], dtq[:, 0:TC], AF.Identity,
                        scale=smalls["c_lnw"][:, db:db + 1],
                        bias=smalls["c_lnb"][:, db:db + 1])), waits=S.after("v"))
                    def gatef(e, sem, db=db, dbl=dbl):
                        e.scalar_tensor_tensor(dtq[:, 0:TC],
                                               xi_t[:, TC * db:TC * (db + 1)],
                                               smalls["c_dprm"][:, db:db + 1],
                                               dtq[:, 0:TC], ALU.mult, ALU.add)
                        e.tensor_tensor(dtq[:, 0:TC], dtq[:, 0:TC], xiraw[:, 0:TC], ALU.mult)
                        e.tensor_copy(gate_h[:, TC * dbl:TC * (dbl + 1)], dtq[:, 0:TC])
                        e.tensor_sub(gate_l[:, TC * dbl:TC * (dbl + 1)], dtq[:, 0:TC],
                                     gate_h[:, TC * dbl:TC * (dbl + 1)])\
                            .then_inc(sem, 1)
                    S.rec("v", gatef, waits=S.after("s"))
                    gate_done = S.cnt["v"]

                if "mtap" in debug:
                    def dmp_g(e, sem, tc=tc, hg=hg):
                        for dbl in range(4):
                            db = 4 * hg + dbl
                            e.dma_start(
                                out=c["dbg_mgh"][128 * db:128 * (db + 1),
                                                 TC * tc:TC * (tc + 1)],
                                in_=gate_h[:, TC * dbl:TC * (dbl + 1)])\
                                .then_inc(sem, 16)
                            e.dma_start(
                                out=c["dbg_mgl"][128 * db:128 * (db + 1),
                                                 TC * tc:TC * (tc + 1)],
                                in_=gate_l[:, TC * dbl:TC * (dbl + 1)])\
                                .then_inc(sem, 16)
                    S.rec("g", dmp_g, waits=S.after("v"), delta=16 * 8)
                    S.rec("v", lambda e, sem: None, waits=S.after("g"), delta=0)
                wgate = S.after("v")
                wo_evac = [0, 0]
                for oc in range(8):
                    bu = oc % 2
                    w = [("t", wstat_done[bu])] if wstat_done[bu] else []
                    S.rec("y", lambda e, sem, oc=oc, bu=bu, hg=hg: (
                        e.dma_start(out=w1h_s[bu][:, 0:512]
                                    .rearrange("p (k c) -> p k c", c=128),
                                    in_=sc["woutT_h"][512 * hg:512 * (hg + 1),
                                                      128 * oc:128 * (oc + 1)]
                                    .rearrange("(k p) c -> p k c", p=128))
                        .then_inc(sem, 16),
                        e.dma_start(out=w1l_s[bu][:, 0:512]
                                    .rearrange("p (k c) -> p k c", c=128),
                                    in_=sc["woutT_l"][512 * hg:512 * (hg + 1),
                                                      128 * oc:128 * (oc + 1)]
                                    .rearrange("(k p) c -> p k c", p=128))
                        .then_inc(sem, 16)), waits=w, delta=32)
                    wy = S.after("y")
                    pbu = pb[2 + bu]
                    wps = [("v", wo_evac[bu])] if wo_evac[bu] else []
                    def mmwo(e, sem, bu=bu, pbu=pbu):
                        for k in range(4):
                            ck = slice(128 * k, 128 * (k + 1))
                            e.matmul(pbu[:, 0:TC], w1h_s[bu][:, ck],
                                     gate_h[:, TC * k:TC * (k + 1)],
                                     start=(k == 0), stop=False)
                            e.matmul(pbu[:, 0:TC], w1h_s[bu][:, ck],
                                     gate_l[:, TC * k:TC * (k + 1)],
                                     start=False, stop=False)
                            inst = e.matmul(pbu[:, 0:TC], w1l_s[bu][:, ck],
                                            gate_h[:, TC * k:TC * (k + 1)],
                                            start=False, stop=(k == 3))
                        inst.then_inc(sem, 1)
                    S.rec("t", mmwo, waits=wy + wps +
                          (wgate if oc == 0 else []))
                    wstat_done[bu] = S.cnt["t"]
                    def xadd(e, sem, oc=oc, pbu=pbu, tsl=tsl):
                        e.tensor_add(x_res[oc][:, tsl], x_res[oc][:, tsl],
                                     pbu[:, 0:TC]).then_inc(sem, 1)
                    S.rec("v", xadd, waits=S.after("t"))
                    wo_evac[bu] = S.cnt["v"]
        st_.close()
        S.barrier()

    c["mamba"] = mamba
    c["free_ffn0_tiles"] = free_ffn0_tiles

    def attention():
        _pa = set(_os2.environ.get("PARA", "").split(","))
        if "1" in _pa or "all" in _pa:
            S.para = True
        def _pon(tag):
            if tag in _pa:
                S.para = True
        def _poff(tag):
            if tag in _pa and "1" not in _pa and "all" not in _pa:
                S.para = False
        if "atap" in debug:
            c["dbg_actx"] = nc.dram_tensor("dbg_actx", [256, T], BF16,
                                           kind="ExternalOutput").ap()
            c["dbg_aav"] = nc.dram_tensor("dbg_aav", [256, T], BF16,
                                          kind="ExternalOutput").ap()
            c["dbg_aqk"] = nc.dram_tensor("dbg_aqk", [512, T], BF16,
                                          kind="ExternalOutput").ap()
        if "dtap" in debug:
            c["dbg_aden"] = nc.dram_tensor("dbg_aden", [8, 4096], F32,
                                           kind="ExternalOutput").ap()
            c["dbg_apf"] = nc.dram_tensor("dbg_apf", [128, 512], F32,
                                          kind="ExternalOutput").ap()
            c["dbg_aps"] = nc.dram_tensor("dbg_aps", [128, 512], F32,
                                          kind="ExternalOutput").ap()
            c["dbg_abnd"] = nc.dram_tensor("dbg_abnd", [8, T], BF16,
                                           kind="ExternalOutput").ap()
        st_ = Stage(nc)
        asb = st_.sb
        qh = asb("qh", [128, T], BF16); ql = asb("ql", [128, T], BF16)
        kh = asb("kh", [128, T], BF16); kl = asb("kl", [128, T], BF16)
        ctxh = asb("ctxh", [128, T], BF16); ctxl = asb("ctxl", [128, T], BF16)
        vtmh = asb("vtmh", [128, 16 * 130], BF16)
        vtml = asb("vtml", [128, 16 * 130], BF16)
        pf = asb("pf", [128, 512], F32)
        ph = asb("ph", [128, 512], BF16); pl = asb("pl", [128, 512], BF16)
        avf = asb("avf", [128, 512], F32)
        avf2 = asb("avf2", [64, 512], F32)
        avh = asb("avh", [128, 512], BF16); avl = asb("avl", [128, 512], BF16)
        avacA = asb("avacA", [65, 512], F32)
        avacB = asb("avacB", [65, 512], F32)
        sq2 = asb("sq2", [128, 512], F32)
        rrow = asb("rrow", [1, 1024], F32)
        csh, csl = ctxh, ctxl
        wvb = [w1h_s[0], w1l_s[0]]

        wbar = S.after("v", "s", "g", "t")

        def proj_pair(dst_h, dst_l, whsrc, wlsrc, col0, scale_col=None, wext=()):
            _pon("proj")
            vdone = {}
            S.rec("y", lambda e, sem: (
                e.dma_start(out=w1h_s[0][:].rearrange("p (k c) -> p k c", c=128),
                            in_=whsrc[:, col0:col0 + 128]
                            .rearrange("(k p) c -> p k c", p=128)).then_inc(sem, 16),
                e.dma_start(out=w1l_s[0][:].rearrange("p (k c) -> p k c", c=128),
                            in_=wlsrc[:, col0:col0 + 128]
                            .rearrange("(k p) c -> p k c", p=128)).then_inc(sem, 16)),
                waits=list(wext) + S.after("t"), delta=32)
            wy = S.after("y")
            for nt in range(4):
                sl = slice(512 * nt, 512 * (nt + 1))
                bu = nt % 2
                wps = ([("v", vdone[nt - 2])] if nt >= 2
                       else list(wext) + S.after("v"))
                def mmp(e, sem, sl=sl, bu=bu):
                    for k in range(8):
                        ck = slice(128 * k, 128 * (k + 1))
                        e.matmul(pb[4 + bu][:], w1h_s[0][:, ck], xn_h[k][:, sl],
                                 start=(k == 0), stop=False)
                        e.matmul(pb[4 + bu][:], w1h_s[0][:, ck], xn_l[k][:, sl],
                                 start=False, stop=False)
                        inst = e.matmul(pb[4 + bu][:], w1l_s[0][:, ck],
                                        xn_h[k][:, sl], start=False, stop=(k == 7))
                    inst.then_inc(sem, 1)
                S.rec("t", mmp, waits=wy + wps)
                wt = S.after("t")
                def pairf(e, sem, bu=bu, sl=sl, dst_h=dst_h, dst_l=dst_l,
                          scale_col=scale_col):
                    if scale_col is None:
                        e.tensor_copy(dst_h[:, sl], pb[4 + bu][:])
                        e.tensor_sub(dst_l[:, sl], pb[4 + bu][:],
                                     dst_h[:, sl]).then_inc(sem, 1)
                    else:
                        e.tensor_single_scalar(dst_h[:, sl], pb[4 + bu][:],
                                               cpool[:, scale_col:scale_col + 1],
                                               ALU.mult)
                        e.scalar_tensor_tensor(dst_l[:, sl], pb[4 + bu][:],
                                               cpool[:, scale_col:scale_col + 1],
                                               dst_h[:, sl], ALU.mult,
                                               ALU.subtract).then_inc(sem, 1)
                S.rec("v", pairf, waits=wt)
                vdone[nt] = S.cnt["v"]
            _poff("proj")

        proj_pair(ctxh, ctxl, sc["weT_h"], sc["weT_l"], 0, scale_col=11,
                  wext=wbar)
        if "atap" in debug:
            def dmp_ctx(e, sem):
                e.dma_start(out=c["dbg_actx"][0:128, :], in_=ctxh[:])\
                    .then_inc(sem, 16)
                e.dma_start(out=c["dbg_actx"][128:256, :], in_=ctxl[:])\
                    .then_inc(sem, 16)
            S.rec("g", dmp_ctx, waits=S.after("v"), delta=32)
            S.rec("t", lambda e, sem: None, waits=S.after("g"), delta=0)

        wctx = S.after("v")

        for hp in range(8):
            hsl = slice(128 * hp, 128 * (hp + 1))
            proj_pair(qh, ql, sc["wqT_h"], sc["wqT_l"], 128 * hp,
                      wext=(wctx if hp == 0 else S.after("t")))
            proj_pair(kh, kl, sc["wkT_h"], sc["wkT_l"], 128 * hp, scale_col=6)
            _pon("vphase")
            def vinit(e, sem):
                e.memset(vtmh[:], 0.0)
                e.memset(vtml[:], 0.0)
                e.memset(vtmh[:].rearrange("p (j c) -> p j c", c=130)[:, :, 64], 1.0)
                e.memset(vtmh[:].rearrange("p (j c) -> p j c", c=130)[:, :, 129],
                         1.0).then_inc(sem, 1)
            S.rec("v", vinit, waits=S.after("t"))
            S.rec("y", lambda e, sem, hsl=hsl: (
                e.dma_start(out=wvb[0][:].rearrange("p (k c) -> p k c", c=128),
                            in_=sc["wvT_h"][:, hsl]
                            .rearrange("(k p) c -> p k c", p=128)).then_inc(sem, 16),
                e.dma_start(out=wvb[1][:].rearrange("p (k c) -> p k c", c=128),
                            in_=sc["wvT_l"][:, hsl]
                            .rearrange("(k p) c -> p k c", p=128)).then_inc(sem, 16)),
                waits=S.after("t"), delta=32)
            wyv = S.after("y")
            vvdone = {}
            for tb in range(16):
                tsl = slice(128 * tb, 128 * (tb + 1))
                bu = tb % 2
                wps = [("v", vvdone[tb - 2])] if tb >= 2 else S.after("v")
                def mmv(e, sem, tsl=tsl, bu=bu):
                    for k in range(8):
                        ck = slice(128 * k, 128 * (k + 1))
                        e.matmul(pb[2 + bu][:, 0:128], xn_h[k][:, tsl],
                                 wvb[0][:, ck], start=(k == 0), stop=False)
                        e.matmul(pb[2 + bu][:, 0:128], xn_h[k][:, tsl],
                                 wvb[1][:, ck], start=False, stop=False)
                        inst = e.matmul(pb[2 + bu][:, 0:128], xn_l[k][:, tsl],
                                        wvb[0][:, ck], start=False, stop=(k == 7))
                    inst.then_inc(sem, 1)
                S.rec("t", mmv, waits=wyv + wps)
                def vpair(e, sem, tb=tb, bu=bu):
                    b0 = 130 * tb
                    e.tensor_copy(vtmh[:, b0:b0 + 64], pb[2 + bu][:, 0:64])
                    e.tensor_sub(vtml[:, b0:b0 + 64], pb[2 + bu][:, 0:64],
                                 vtmh[:, b0:b0 + 64])
                    e.tensor_copy(vtmh[:, b0 + 65:b0 + 129], pb[2 + bu][:, 64:128])
                    e.tensor_sub(vtml[:, b0 + 65:b0 + 129], pb[2 + bu][:, 64:128],
                                 vtmh[:, b0 + 65:b0 + 129]).then_inc(sem, 1)
                S.rec("v", vpair, waits=S.after("t"))
                vvdone[tb] = S.cnt["v"]
            _poff("vphase")
            if "atap" in debug and hp == 0:
                def dmp_qk(e, sem):
                    e.dma_start(out=c["dbg_aqk"][0:128, :], in_=qh[:])\
                        .then_inc(sem, 16)
                    e.dma_start(out=c["dbg_aqk"][128:256, :], in_=ql[:])\
                        .then_inc(sem, 16)
                    e.dma_start(out=c["dbg_aqk"][256:384, :], in_=kh[:])\
                        .then_inc(sem, 16)
                    e.dma_start(out=c["dbg_aqk"][384:512, :], in_=kl[:])\
                        .then_inc(sem, 16)
                S.rec("g", dmp_qk, waits=S.after("v"), delta=16 * 4)
                S.rec("t", lambda e, sem: None, waits=S.after("g"), delta=0)
            wbnd = S.after("v")
            S.rec("y", lambda e, sem, hsl=hsl: (
                e.dma_start(out=w1h_s[0][:].rearrange("p (k c) -> p k c", c=128),
                            in_=sc["woT_h"][hsl, :]
                            .rearrange("p (k c) -> p k c", c=128)).then_inc(sem, 16),
                e.dma_start(out=w1l_s[0][:].rearrange("p (k c) -> p k c", c=128),
                            in_=sc["woT_l"][hsl, :]
                            .rearrange("p (k c) -> p k c", c=128)).then_inc(sem, 16)),
                waits=S.after("t"), delta=32)
            wyo = S.after("y")
            for qt in range(4):
                qsl = slice(512 * qt, 512 * (qt + 1))
                sdone = {}
                avdone = 0
                _pon("scores")
                for kc in range(16):
                    ksl = slice(128 * kc, 128 * (kc + 1))
                    pA = pb[0 + 2 * (kc % 2)]
                    pB = pb[1 + 2 * (kc % 2)]
                    wps = ([("s", sdone[kc - 2])] if kc >= 2 else
                           (wbnd + ([("v", avdone)] if avdone else [])))
                    def mms(e, sem, qsl=qsl, ksl=ksl, pA=pA, pB=pB):
                        for (pS, r0, r1) in ((pA, 0, 64), (pB, 64, 128)):
                            e.matmul(pS[:], kh[r0:r1, ksl], qh[r0:r1, qsl],
                                     start=True, stop=False)
                            e.matmul(pS[:], kh[r0:r1, ksl], ql[r0:r1, qsl],
                                     start=False, stop=False)
                            e.matmul(pS[:], kl[r0:r1, ksl], qh[r0:r1, qsl],
                                     start=False, stop=False)
                            e.matmul(pS[:], csh[:, ksl], ctxh[:, qsl],
                                     start=False, stop=False)
                            e.matmul(pS[:], csh[:, ksl], ctxl[:, qsl],
                                     start=False, stop=False)
                            e.matmul(pS[:], csl[:, ksl], ctxh[:, qsl],
                                     start=False, stop=True).then_inc(sem, 1)
                    S.rec("t", mms, waits=wps, delta=2)
                    wmm = S.after("t")
                    for hh, (pS, pav) in enumerate(((pA, pb[4]), (pB, pb[5]))):
                        S.rec("s", inc1(lambda e, pS=pS: e.activation(
                            pf[:], pS[:], AF.Exp)),
                            waits=wmm + S.after("v") + S.after("t"))
                        sdone[kc] = S.cnt["s"]
                        if ("atap" in debug and hp == 0 and qt == 0
                                and kc == 3 and hh == 0
                                and _os2.environ.get("ATT_PS") == "1"):
                            S.rec("v", inc1(lambda e, pS=pS: e.tensor_copy(
                                avf[:], pS[:])), waits=wmm)
                            S.rec("g", lambda e, sem: e.dma_start(
                                out=c["dbg_aps"][:, :], in_=avf[:])
                                .then_inc(sem, 16), waits=S.after("v"),
                                delta=16)
                            S.rec("v", lambda e, sem: None,
                                  waits=S.after("g"), delta=0)
                            S.rec("g", lambda e, sem: e.dma_start(
                                out=c["dbg_apf"][:, :], in_=pf[:])
                                .then_inc(sem, 16), waits=S.after("s"),
                                delta=16)
                            S.rec("v", lambda e, sem: None,
                                  waits=S.after("g"), delta=0)
                        def ppair(e, sem):
                            e.tensor_copy(ph[:], pf[:])
                            e.tensor_sub(pl[:], pf[:], ph[:]).then_inc(sem, 1)
                        S.rec("v", ppair, waits=S.after("s") + S.after("t"))
                        b0 = 130 * kc + 65 * hh
                        def mmav(e, sem, b0=b0, pav=pav):
                            e.matmul(pav[0:65, :], vtmh[:, b0:b0 + 65], ph[:],
                                     start=True, stop=False)
                            e.matmul(pav[0:65, :], vtml[:, b0:b0 + 65], ph[:],
                                     start=False, stop=False)
                            e.matmul(pav[0:65, :], vtmh[:, b0:b0 + 65], pl[:],
                                     start=False, stop=True).then_inc(sem, 1)
                        S.rec("t", mmav, waits=S.after("v"))
                        avac = avacA if hh == 0 else avacB
                        def avadd(e, sem, pav=pav, avac=avac, kc=kc):
                            if kc == 0:
                                e.tensor_copy(avac[0:65, :],
                                              pav[0:65, :]).then_inc(sem, 1)
                            else:
                                e.tensor_add(avac[0:65, :], avac[0:65, :],
                                             pav[0:65, :]).then_inc(sem, 1)
                        S.rec("v", avadd, waits=S.after("t"))
                _poff("scores")
                _pon("avwo")
                wtav = S.after("t")
                if "dtap" in debug:
                    def dmp_den(e, sem, hp=hp, qt=qt):
                        e.dma_start(
                            out=c["dbg_aden"][hp:hp + 1,
                                              1024 * qt:1024 * qt + 512],
                            in_=avacA[64:65, :]).then_inc(sem, 16)
                        e.dma_start(
                            out=c["dbg_aden"][hp:hp + 1,
                                              1024 * qt + 512:1024 * (qt + 1)],
                            in_=avacB[64:65, :]).then_inc(sem, 16)
                    S.rec("g", dmp_den, waits=S.after("v"), delta=32)
                S.rec("v", inc1(lambda e: (
                    e.reciprocal(rrow[0:1, 0:512], avacA[64:65, :]),
                    e.reciprocal(rrow[0:1, 512:1024], avacB[64:65, :]))[-1]),
                    waits=wtav)
                def recbc(e, sem):
                    e.matmul(pb[6][0:64, :], ones_f[0:1, 0:64], rrow[0:1, 0:512],
                             start=True, stop=True).then_inc(sem, 1)
                    e.matmul(pb[7][64:128, :], ones_f[0:1, 0:64],
                             rrow[0:1, 512:1024],
                             start=True, stop=True).then_inc(sem, 1)
                    e.matmul(pb[6][64:128, :], ident[0:64, 0:64],
                             avacB[0:64, :],
                             start=True, stop=True).then_inc(sem, 1)
                S.rec("t", recbc, waits=S.after("v") + S.after("s"), delta=3)
                def avmove(e, sem):
                    e.tensor_copy(avf[0:64, :], avacA[0:64, :])
                    e.tensor_copy(avf[64:128, :], pb[6][64:128, :])
                    e.tensor_copy(sq2[0:64, :], pb[6][0:64, :])
                    e.tensor_copy(sq2[64:128, :],
                                  pb[7][64:128, :]).then_inc(sem, 1)
                S.rec("v", avmove, waits=S.after("t"))
                def avpair(e, sem):
                    e.tensor_tensor(avf[:], avf[:], sq2[:], ALU.mult)
                    e.tensor_copy(avh[:], avf[:])
                    e.tensor_sub(avl[:], avf[:], avh[:]).then_inc(sem, 1)
                S.rec("v", avpair, waits=S.after("v"))
                wav = S.after("v")
                if "atap" in debug and hp == 0:
                    def dmp_av(e, sem, qt=qt):
                        e.dma_start(out=c["dbg_aav"][0:128,
                                                     512 * qt:512 * (qt + 1)],
                                    in_=avh[:]).then_inc(sem, 16)
                        e.dma_start(out=c["dbg_aav"][128:256,
                                                     512 * qt:512 * (qt + 1)],
                                    in_=avl[:]).then_inc(sem, 16)
                    S.rec("g", dmp_av, waits=wav, delta=32)
                    S.rec("t", lambda e, sem: None, waits=S.after("g"), delta=0)
                odone = {}
                for oc in range(8):
                    pwo = pb[2 + oc % 2]
                    wps2 = [("v", odone[oc - 2])] if oc >= 2 else S.after("v")
                    def mmo(e, sem, oc=oc, pwo=pwo):
                        ck = slice(128 * oc, 128 * (oc + 1))
                        e.matmul(pwo[:], w1h_s[0][:, ck], avh[:],
                                 start=True, stop=False)
                        e.matmul(pwo[:], w1h_s[0][:, ck], avl[:],
                                 start=False, stop=False)
                        e.matmul(pwo[:], w1l_s[0][:, ck], avh[:],
                                 start=False, stop=True).then_inc(sem, 1)
                    S.rec("t", mmo, waits=wyo + wav + wps2)
                    def xacc(e, sem, oc=oc, pwo=pwo, qsl=qsl):
                        e.tensor_add(x_res[oc][:, qsl], x_res[oc][:, qsl],
                                     pwo[:]).then_inc(sem, 1)
                    S.rec("v", xacc, waits=S.after("t"))
                    odone[oc] = S.cnt["v"]
                avdone = S.cnt["v"]
                _poff("avwo")
        if "ptap" in debug:
            pt = nc.dram_tensor("dbg_ptap", [12 * 128, T], BF16,
                                kind="ExternalOutput").ap()
            c["dbg_ptap"] = pt
            def dmp_post(e, sem):
                for i, t_ in enumerate((qh, ql, kh, kl, ctxh, ctxl)):
                    e.dma_start(out=pt[128 * i:128 * (i + 1), :],
                                in_=t_[:]).then_inc(sem, 16)
                e.dma_start(out=pt[128 * 8:128 * 9, 0:512],
                            in_=avh[:]).then_inc(sem, 16)
                e.dma_start(out=pt[128 * 9:128 * 10, 0:512],
                            in_=avl[:]).then_inc(sem, 16)
                e.dma_start(out=pt[128 * 10:128 * 11, 0:2048],
                            in_=vtmh[:, 0:2048]).then_inc(sem, 16)
                e.dma_start(out=pt[128 * 11:128 * 12, 0:2048],
                            in_=vtml[:, 0:2048]).then_inc(sem, 16)
            S.rec("g", dmp_post, waits=S.after("v", "s", "t", "g"),
                  delta=16 * 10)
            S.rec("v", lambda e, sem: None, waits=S.after("g"), delta=0)
        st_.close()
        S.barrier()
        S.para = False

    c["attention"] = attention

    def pmax_rep(src, scratch4, scratchT, pbA, pbB, wpre):
        def tr4(e, sem):
            for j in range(4):
                inst = e.transpose(pbA[:, 128 * j:128 * (j + 1)],
                                   src[:, 128 * j:128 * (j + 1)], ident[:])
            inst.then_inc(sem, 1)
        S.rec("t", tr4, waits=wpre)
        def red4(e, sem):
            for j in range(4):
                inst = e.reduce_max(scratch4[:, j:j + 1],
                                    pbA[:, 128 * j:128 * (j + 1)], axis=AX.X)
            inst.then_inc(sem, 1)
        S.rec("v", red4, waits=S.after("t"))
        S.rec("t", inc1(lambda e: e.transpose(pbB[0:4, 0:128], scratch4[:],
                                              ident[:])), waits=S.after("v"))
        S.rec("v", inc1(lambda e: e.tensor_copy(scratchT[0:4, 0:128],
                                                pbB[0:4, 0:128])),
              waits=S.after("t"))
        def bc4(e, sem):
            for j in range(4):
                inst = e.matmul(pbA[:, 128 * j:128 * (j + 1)],
                                ident[0:4, j:j + 1].broadcast_to([4, 128]),
                                scratchT[0:4, 0:128], start=True, stop=True)
            inst.then_inc(sem, 1)
        S.rec("t", bc4, waits=S.after("v"))
        S.rec("v", inc1(lambda e: e.tensor_copy(src[:], pbA[:])),
              waits=S.after("t"))

    def ffn1():
        st_ = Stage(nc)
        xib = [st_.sb(f"xib{i}", [128, 512], BF16) for i in range(8)]
        mrow = st_.sb("f1_mrow", [128, 512], F32)
        invx = st_.sb("f1_invx", [128, 512], F32)
        sclx = st_.sb("f1_sclx", [128, 512], F32)
        xif = st_.sb("f1_xif", [128, 512], F32)
        pwi2 = st_.sb("f1_i32", [128, 512], I32)
        gu = st_.sb("f1_gu", [128, 32 * 512], BF16)
        gt = st_.sb("f1_gt", [128, 512], F32)
        ut = st_.sb("f1_ut", [128, 512], F32)
        m2 = st_.sb("f1_m2", [128, 512], F32)
        i2 = st_.sb("f1_i2", [128, 512], F32)
        mrow2 = st_.sb("f1_mrow2", [128, 512], F32)
        invx2 = st_.sb("f1_invx2", [128, 512], F32)
        sclx2 = st_.sb("f1_sclx2", [128, 512], F32)
        wgb = [w1h_s[0], w1h_s[1]]
        wub = [w1l_s[0], w1l_s[1]]

        for nt in range(4):
            sl = slice(512 * nt, 512 * (nt + 1))
            wbar = S.after("v", "s", "g", "t")
            def amax(e, sem, sl=sl):
                for cc in range(8):
                    e.tensor_add(sclx[:], xn_h[cc][:, sl], xn_l[cc][:, sl])
                    e.tensor_scalar_mul(invx[:], sclx[:], -1.0)
                    e.tensor_max(invx[:], invx[:], sclx[:])
                    if cc == 0:
                        inst = e.tensor_copy(mrow[:], invx[:])
                    else:
                        inst = e.tensor_max(mrow[:], mrow[:], invx[:])
                inst.then_inc(sem, 1)
            S.rec("v", amax, waits=wbar)
            pmax_rep(mrow, i2[:, 0:4], xif[0:4, 0:128], pb[4], pb[5],
                     S.after("v"))
            def mkscale(e, sem):
                e.tensor_scalar_max(mrow[:], mrow[:], 1e-5)
                e.tensor_scalar_mul(invx[:], mrow[:], 1.0 / 127.0)
                e.reciprocal(sclx[:], invx[:]).then_inc(sem, 1)
            S.rec("v", mkscale, waits=S.after("v"))
            for cc in range(8):
                def qx(e, sem, cc=cc, sl=sl):
                    e.tensor_add(xif[:], xn_h[cc][:, sl], xn_l[cc][:, sl])
                    e.tensor_tensor(xif[:], xif[:], sclx[:], ALU.mult)
                    e.tensor_copy(pwi2[:], xif[:])
                    e.tensor_copy(xib[cc][:], pwi2[:]).then_inc(sem, 1)
                S.rec("v", qx)
            wqx = S.after("v")
            mm_done = {}
            gu_done = {}
            for ocf in range(32):
                bu = ocf % 2
                w = [("t", mm_done[ocf - 2])] if ocf >= 2 else list(wqx)
                S.rec("y", lambda e, sem, ocf=ocf, bu=bu: (
                    e.dma_start(out=wgb[bu][:].rearrange("p (k c) -> p k c", c=128),
                                in_=sc["wgT_i"][:, 128 * ocf:128 * (ocf + 1)]
                                .rearrange("(k p) c -> p k c", p=128))
                    .then_inc(sem, 16),
                    e.dma_start(out=wub[bu][:].rearrange("p (k c) -> p k c", c=128),
                                in_=sc["wuT_i"][:, 128 * ocf:128 * (ocf + 1)]
                                .rearrange("(k p) c -> p k c", p=128))
                    .then_inc(sem, 16)), waits=w, delta=32)
                wy = S.after("y")
                pg = pb[bu]
                pu = pb[2 + bu]
                wps = [("v", gu_done[ocf - 2])] if ocf >= 2 else []
                def mmgu(e, sem, bu=bu, pg=pg, pu=pu):
                    for k in range(8):
                        ck = slice(128 * k, 128 * (k + 1))
                        e.matmul(pg[:], wgb[bu][:, ck], xib[k][:],
                                 start=(k == 0), stop=(k == 7))
                    for k in range(8):
                        ck = slice(128 * k, 128 * (k + 1))
                        inst = e.matmul(pu[:], wub[bu][:, ck], xib[k][:],
                                        start=(k == 0), stop=(k == 7))
                    inst.then_inc(sem, 1)
                S.rec("t", mmgu, waits=wy + wps)
                mm_done[ocf] = S.cnt["t"]
                wt = S.after("t")
                S.rec("s", inc1(
                    lambda e, pg=pg: e.activation(gt[:], pg[:], AF.Identity,
                                                  scale=scale_g[:, 1:2]),
                    lambda e, pu=pu: e.activation(ut[:], pu[:], AF.Identity,
                                                  scale=scale_u[:, 1:2])),
                    waits=wt + S.after("v"))
                def vmul(e, sem):
                    e.tensor_tensor(gt[:], gt[:], invx[:], ALU.mult)
                    e.tensor_tensor(ut[:], ut[:], invx[:],
                                    ALU.mult).then_inc(sem, 1)
                S.rec("v", vmul, waits=S.after("s"))
                S.rec("s", inc1(lambda e: e.activation(gt[:], gt[:], AF.Silu)),
                      waits=S.after("v"))
                def guprod(e, sem, ocf=ocf):
                    e.tensor_tensor(m2[:], gt[:], ut[:], ALU.mult)
                    e.tensor_copy(gu[:, 512 * ocf:512 * (ocf + 1)], m2[:])
                    e.tensor_scalar_mul(i2[:], m2[:], -1.0)
                    e.tensor_max(i2[:], i2[:], m2[:])
                    if ocf == 0:
                        inst = e.tensor_copy(mrow2[:], i2[:])
                    else:
                        inst = e.tensor_max(mrow2[:], mrow2[:], i2[:])
                    inst.then_inc(sem, 1)
                S.rec("v", guprod, waits=S.after("s"))
                gu_done[ocf] = S.cnt["v"]
            pmax_rep(mrow2, i2[:, 0:4], xif[0:4, 0:128], pb[4], pb[5],
                     S.after("v"))
            def mkscale2(e, sem):
                e.tensor_scalar_max(mrow2[:], mrow2[:], 1e-5)
                e.tensor_scalar_mul(invx2[:], mrow2[:], 1.0 / 127.0)
                e.reciprocal(sclx2[:], invx2[:]).then_inc(sem, 1)
            S.rec("v", mkscale2, waits=S.after("v"))
            for ocf in range(32):
                def qgu(e, sem, ocf=ocf):
                    osl = slice(512 * ocf, 512 * (ocf + 1))
                    e.tensor_tensor(m2[:], gu[:, osl], sclx2[:], ALU.mult)
                    e.tensor_copy(pwi2[:], m2[:])
                    e.tensor_copy(gu[:, osl], pwi2[:]).then_inc(sem, 1)
                S.rec("v", qgu)
            wq2 = S.after("v")
            for half in range(2):
                dn_done = {}
                prev_t = S.after("t")
                for ocf in range(32):
                    bu = ocf % 2
                    w = ([("t", dn_done[ocf - 2])] if ocf >= 2
                         else list(wq2) + S.after("s") + list(prev_t))
                    S.rec("y", lambda e, sem, ocf=ocf, bu=bu, half=half:
                          e.dma_start(out=wgb[bu][:, 0:512],
                                      in_=sc["wdT_i"][128 * ocf:128 * (ocf + 1),
                                                      512 * half:512 * (half + 1)])
                          .then_inc(sem, 16), waits=w, delta=16)
                    wy = S.after("y")
                    def mmd(e, sem, ocf=ocf, bu=bu):
                        osl = slice(512 * ocf, 512 * (ocf + 1))
                        for d4 in range(4):
                            ck = slice(128 * d4, 128 * (d4 + 1))
                            inst = e.matmul(pb[4 + d4][:], wgb[bu][:, ck],
                                            gu[:, osl], start=(ocf == 0),
                                            stop=(ocf == 31))
                        inst.then_inc(sem, 1)
                    S.rec("t", mmd, waits=wy + (S.after("v") if ocf == 0 else []))
                    dn_done[ocf] = S.cnt["t"]
                wt = S.after("t")
                for d4 in range(4):
                    cc = 4 * half + d4
                    S.rec("s", inc1(lambda e, d4=d4: e.activation(
                        m2[:], pb[4 + d4][:], AF.Identity, scale=scale_d[:, 1:2])),
                        waits=wt + S.after("v"))
                    def fin(e, sem, cc=cc, sl=sl):
                        e.tensor_tensor(i2[:], m2[:], invx2[:], ALU.mult)
                        e.tensor_add(x_res[cc][:, sl], x_res[cc][:, sl],
                                     i2[:]).then_inc(sem, 1)
                    S.rec("v", fin, waits=S.after("s"))
        st_.close()
        S.barrier()

    c["ffn1"] = ffn1

    def store_out():
        wbar = S.after("v", "s", "t", "g")
        st_ = Stage(nc)
        xout = st_.sb("xout", [128, 1024], F32)
        for i in range(16):
            def trb(e, sem, i=i):
                for cc in range(8):
                    inst = e.transpose(pb[cc][:, 0:128],
                                       x_res[cc][:, 128 * i:128 * (i + 1)],
                                       ident[:])
                inst.then_inc(sem, 1)
            S.rec("t", trb, waits=(list(wbar) if i == 0 else []) + S.after("v"))
            def cpy(e, sem):
                for cc in range(8):
                    inst = e.tensor_copy(xout[:, 128 * cc:128 * (cc + 1)],
                                         pb[cc][:, 0:128])
                inst.then_inc(sem, 1)
            S.rec("v", cpy, waits=S.after("t") + S.after("g"))
            S.rec("g", lambda e, sem, i=i: e.dma_start(
                out=out_d[128 * i:128 * (i + 1), :], in_=xout[:])
                .then_inc(sem, 16), waits=S.after("v"), delta=16)
        S.rec("g", lambda e, sem: e.wait_ge(sem, S.cnt["g"]), delta=0)

    c["store_out"] = store_out

    dbg_outs = {}

    def dbg_dump(tag):
        if tag not in debug:
            return
        dten = nc.dram_tensor(f"dbg_{tag}", [8 * 128, T], F32,
                              kind="ExternalOutput").ap()
        dbg_outs[tag] = dten
        S.rec("v", inc1(lambda e: e.tensor_copy(cpool[:, 15:16],
                                                cpool[:, 1:2])),
              waits=S.after("t", "s", "g"))
        wb = S.after("v", "s", "t", "g")
        def dump(e, sem):
            for cc in range(8):
                e.dma_start(out=dten[128 * cc:128 * (cc + 1), :],
                            in_=x_res[cc][:]).then_inc(sem, 16)
        S.rec("g", dump, waits=wb, delta=16 * 8)

    def dbg_dump_xn(tag):
        if tag not in debug:
            return
        dten = nc.dram_tensor(f"dbg_{tag}", [8 * 128, 2 * T], F32,
                              kind="ExternalOutput").ap()
        dbg_outs[tag] = dten
        wb = S.after("v", "s", "t", "g")
        def dump(e, sem):
            for cc in range(8):
                e.dma_start(out=dten[128 * cc:128 * (cc + 1), 0:T],
                            in_=xn_h[cc][:]).then_inc(sem, 16)
                e.dma_start(out=dten[128 * cc:128 * (cc + 1), T:2 * T],
                            in_=xn_l[cc][:]).then_inc(sem, 16)
        S.rec("g", dump, waits=wb, delta=32 * 8)

    import os as _os
    load_x()
    dbg_dump("x0")
    xtm_stage.close()
    S.barrier()
    def _maybe_stop(tag):
        if _os.environ.get("STOP_AFTER") == tag:
            S.rec("g", lambda e, sem: e.wait_ge(sem, S.cnt["g"]), delta=0)
            S.emit()
            c["nc"] = nc
            return True
        return False
    if _maybe_stop("load"):
        return nc
    ln_feature_major(x_res, 8, xn_h, xn_l, S.after("v", "t"))
    dbg_dump_xn("xn1")
    dbg_dump("xa")
    if _maybe_stop("ln"):
        return nc
    c["prep_stage"] = Stage(nc)
    wtmp = [c["prep_stage"].sb(f"wtmp{i}", [128, 2048], F32) for i in range(2)]
    whib = [c["prep_stage"].sb(f"whib{i}", [128, 2048], BF16) for i in range(2)]
    wlob = [c["prep_stage"].sb(f"wlob{i}", [128, 2048], BF16) for i in range(2)]
    _sel = _os.environ.get("PREP_SEL", "all")
    _pp = [("w_w1", "w1_h", "w1_l", D, FF, False),
           ("w_w2", "w2_h", "w2_l", FF, D, True),
           ("w_winT", "winT_h", "winT_l", D, 2 * DI, False),
           ("w_woutT", "woutT_h", "woutT_l", DI, D, False),
           ("w_wqT", "wqT_h", "wqT_l", D, D, False),
           ("w_wkT", "wkT_h", "wkT_l", D, D, False),
           ("w_wvT", "wvT_h", "wvT_l", D, D, False),
           ("w_woT", "woT_h", "woT_l", D, D, False),
           ("w_weT", "weT_h", "weT_l", D, DE, False)]
    for _i, (a_, b_, c_, d_, e_, f_) in enumerate(_pp):
        if _sel == "all" or str(_i) in _sel.split(","):
            prep_pair(a_, b_, c_, d_, e_, f16=f_)
    if _sel == "all" or "int" in _sel:
        prep_int("w_wgT", "wgT_i", scale_g, D, FF)
        prep_int("w_wuT", "wuT_i", scale_u, D, FF)
        prep_int("w_wdT", "wdT_i", scale_d, FF, D)
    free_prep_bufs()
    S.barrier()
    dbg_dump("xb")
    if _maybe_stop("prep"):
        return nc
    ffn0()
    free_ffn0_tiles()
    S.barrier()
    dbg_dump("ffn0")
    if _maybe_stop("ffn0"):
        return nc
    mamba()
    dbg_dump("mamba")
    if _maybe_stop("mamba"):
        return nc
    ln_feature_major(x_res, 8, xn_h, xn_l, S.after("v", "s", "t", "g"))
    dbg_dump_xn("xn2")
    front_stage.close()
    S.barrier()
    attention()
    dbg_dump("attn")
    if _maybe_stop("attn"):
        return nc
    ffn1()
    dbg_dump("ffn1")
    if _maybe_stop("ffn1"):
        return nc
    store_out()

    S.emit()
    c["nc"] = nc
    return nc



_prog_cache = {}


def kernel(**inputs):
    from concourse.bass_utils import run_bass_kernel_spmd
    shared, xs = host_layout(inputs)
    if "nc" not in _prog_cache:
        _prog_cache["nc"] = build_program()
    nc = _prog_cache["nc"]
    in_maps = []
    for b_ in range(B):
        m = {"x": xs[b_]}
        m.update(shared)
        in_maps.append(m)
    res = run_bass_kernel_spmd(nc, in_maps, core_ids=list(range(B)))
    out = np.stack([res.results[b_]["out"] for b_ in range(B)], axis=0)
    return out.astype(np.float32)
